# version 10 (final)
# baseline (speedup 1.0000x reference)
import numpy as np
import ml_dtypes
import concourse.bass as bass
import concourse.mybir as mybir

F32 = mybir.dt.float32
BF16 = mybir.dt.bfloat16
AF = mybir.ActivationFunctionType
ALU = mybir.AluOpType

D = 2048
H = 8192
KV = 4096
T = 512
NC_TOK = 4096
SCALE = 1.0 / float(np.sqrt(128.0))
FWHT_SCALE = 1.0 / float(np.sqrt(8192.0))
EPS = 1e-5

ENGS = ("pe", "act", "dve", "gp", "sy")


def sylvester(n):
    Hm = np.array([[1.0]], np.float32)
    while Hm.shape[0] < n:
        Hm = np.block([[Hm, Hm], [Hm, -Hm]]).astype(np.float32)
    return Hm


def host_consts():
    H128 = sylvester(128).astype(ml_dtypes.bfloat16)
    H64 = sylvester(64)
    BD = np.zeros((128, 128), np.float32)
    BD[0:64, 0:64] = H64
    BD[64:128, 64:128] = H64
    return H128, BD.astype(ml_dtypes.bfloat16)


def prep_host(inputs):
    x = np.asarray(inputs["L0_attn_query"], np.float32)
    xT = np.ascontiguousarray(x.reshape(32768, D).T)
    wqT = np.ascontiguousarray(np.asarray(inputs["L0_attn_Wq"], np.float32).T)
    woT = np.ascontiguousarray(np.asarray(inputs["L0_attn_Wo"], np.float32).T)
    kT = np.ascontiguousarray(np.asarray(inputs["L0_attn_kv_k"], np.float32).reshape(KV, D).T)
    vN = np.ascontiguousarray(np.asarray(inputs["L0_attn_kv_v"], np.float32).reshape(KV, D))
    w1Tp = np.ascontiguousarray(
        np.asarray(inputs["L0_ffn_W1"], np.float32).T.reshape(D, 128, 64).transpose(0, 2, 1).reshape(D, H))
    w3Tp = np.ascontiguousarray(
        np.asarray(inputs["L0_ffn_W3"], np.float32).T.reshape(D, 128, 64).transpose(0, 2, 1).reshape(D, H))
    w2T = np.ascontiguousarray(np.asarray(inputs["L0_ffn_W2"], np.float32).T)
    H128c, BDc = host_consts()
    shared = dict(wqT=wqT, woT=woT, kT=kT, vN=vN, w1Tp=w1Tp, w3Tp=w3Tp, w2T=w2T,
                  h128=H128c, bd64=BDc)
    return xT, shared


def build(NG=8):
    nc = bass.Bass("TRN2", target_bir_lowering=False)

    xTg = nc.dram_tensor("xT", [D, NC_TOK], F32, kind="ExternalInput").ap()
    wqT = nc.dram_tensor("wqT", [D, D], F32, kind="ExternalInput").ap()
    woT = nc.dram_tensor("woT", [D, D], F32, kind="ExternalInput").ap()
    kT = nc.dram_tensor("kT", [D, KV], F32, kind="ExternalInput").ap()
    vN = nc.dram_tensor("vN", [KV, D], F32, kind="ExternalInput").ap()
    w1Tp = nc.dram_tensor("w1Tp", [D, H], F32, kind="ExternalInput").ap()
    w3Tp = nc.dram_tensor("w3Tp", [D, H], F32, kind="ExternalInput").ap()
    w2T = nc.dram_tensor("w2T", [H, D], F32, kind="ExternalInput").ap()
    h128_in = nc.dram_tensor("h128", [128, 128], BF16, kind="ExternalInput").ap()
    bd64_in = nc.dram_tensor("bd64", [128, 128], BF16, kind="ExternalInput").ap()

    outT = nc.dram_tensor("outT", [D, NC_TOK], F32, kind="ExternalOutput").ap()

    wq_b = nc.dram_tensor("wq_b", [D, D], BF16, kind="Internal").ap()
    wo_b = nc.dram_tensor("wo_b", [D, D], BF16, kind="Internal").ap()
    k_b = nc.dram_tensor("k_b", [D, KV], BF16, kind="Internal").ap()
    v_b = nc.dram_tensor("v_b", [D, KV], BF16, kind="Internal").ap()
    w1_b = nc.dram_tensor("w1_b", [H, D], BF16, kind="Internal").ap()
    w3_b = nc.dram_tensor("w3_b", [H, D], BF16, kind="Internal").ap()
    w2e = nc.dram_tensor("w2e", [D, H], BF16, kind="Internal").ap()

    from contextlib import ExitStack
    with ExitStack() as stk:
        E = stk.enter_context
        xT_sb = E(nc.sbuf_tensor("xT_sb", [128, 16 * T], F32))
        xn_sb = E(nc.sbuf_tensor("xn_sb", [128, 16 * T], BF16))
        res_sb = E(nc.sbuf_tensor("res_sb", [128, 16 * T], F32))
        q_sb = E(nc.sbuf_tensor("q_sb", [128, 16 * T], BF16))
        arena = E(nc.sbuf_tensor("arena", [128, 64 * T], BF16))
        wbuf = E(nc.sbuf_tensor("wbuf", [128, 4 * 2048], BF16))
        xsq_t = E(nc.sbuf_tensor("xsq_t", [128, 2 * T], BF16))
        sscr = E(nc.sbuf_tensor("sscr", [128, 2 * T], F32))
        hid_t = E(nc.sbuf_tensor("hid_t", [128, 2 * T], BF16))
        outb = E(nc.sbuf_tensor("outb", [128, 2 * T], F32))
        ones128 = E(nc.sbuf_tensor("ones128", [128, 128], BF16))
        eps128 = E(nc.sbuf_tensor("eps128", [128, 1], F32))
        h128_s = E(nc.sbuf_tensor("h128_s", [128, 128], BF16))
        bd64_s = E(nc.sbuf_tensor("bd64_s", [128, 128], BF16))
        accp = E(nc.sbuf_tensor("accp", [128, T], F32))
        psbf = E(nc.sbuf_tensor("psbf", [128, T], BF16))
        prb = E(nc.sbuf_tensor("prb", [128, 2 * T], BF16))
        qrb = E(nc.sbuf_tensor("qrb", [128, 2 * T], BF16))

        psA = E(nc.psum_tensor("psA", [128, 1024], F32))
        psB = E(nc.psum_tensor("psB", [128, 1024], F32))
        psC = E(nc.psum_tensor("psC", [128, 512], F32))
        psD = E(nc.psum_tensor("psD", [128, 512], F32))
        psE = E(nc.psum_tensor("psE", [128, 512], F32))
        psA0, psA1 = psA[:, 0:512], psA[:, 512:1024]

        SEM_NAMES = ("P", "A", "V", "CQ", "XW", "XK", "XV", "XO", "X1", "X3",
                     "W0", "W1", "W2", "W3", "K0", "K1", "K2", "F0", "F1",
                     "FS0", "FS1", "XT", "WE0", "WE1", "OB0", "OB1", "Y2", "Y3", "PL",
                     "Z0", "Z1", "Z2", "Z3", "L0", "L1", "L2")
        SEM = {k: E(nc.semaphore("sem_" + k)) for k in SEM_NAMES}

        CTX0 = 0
        P0 = 8192
        KVS = [10240, 14336, 18432]

        EV = {e: [] for e in ENGS}
        cnt = {k: 0 for k in SEM}
        waited = {e: {k: 0 for k in SEM} for e in ENGS}
        implied = {e: {k: 0 for k in SEM} for e in ENGS}
        free_req = {}

        def emit(eng, f):
            EV[eng].append(f)

        def wait(eng, sem, val):
            if val <= 0 or val <= max(waited[eng][sem], implied[eng][sem]):
                return
            waited[eng][sem] = val
            s = SEM[sem]
            emit(eng, lambda e, s=s, v=val: e.wait_ge(s, v))

        def inc(eng, sem, f, amt=1):
            cnt[sem] += amt
            v = cnt[sem]
            s = SEM[sem]
            emit(eng, lambda e, s=s, a=amt, f=f: f(e).then_inc(s, a))
            if sem in ("P", "A", "V"):
                implied[eng][sem] = v
            return v

        def use(eng, key):
            r = free_req.get(key)
            if r is not None:
                for sem, val in r:
                    wait(eng, sem, val)

        def drain(eng):
            emit(eng, lambda e: e.drain())

        def release(key, sem, val):
            free_req[key] = [(sem, val)]

        def release_multi(key, pairs):
            free_req[key] = list(pairs)

        emit("dve", lambda e: e.memset(ones128[:], 1.0))
        v_consts = inc("dve", "V", lambda e: e.memset(eps128[:], EPS))
        inc("gp", "CQ", lambda e: e.dma_start(out=h128_s[:], in_=h128_in), amt=16)
        g_consts = inc("gp", "CQ", lambda e: e.dma_start(out=bd64_s[:], in_=bd64_in), amt=16)

        cv = {"wq": 0, "k": 0, "v": 0, "wo": 0, "w1": 0, "w3": 0}

        fold = {"ld": {}, "mm": {}, "cp": {}, "stv": {}, "last_store": 0}

        last = {"xn_ready": 0, "q_evac": {}, "scores_last": 0, "w2e_mm_last": 0,
                "wo_mm_last": 0, "wo_evac_last": 0, "epi_last": 0, "lnexp": {},
                "ctx_mul": {}, "u_mm_last": 0, "stats_mm": 0, "pv15": {}}

        def fold_load(q):
            if q < 2:
                wait("gp", "P", last["scores_last"])
            use("gp", ("foldin", q % 2))
            dst = q_sb[:, 4096 + (q % 2) * 2048: 4096 + (q % 2) * 2048 + 2048]
            src = w2T[q * 128:(q + 1) * 128, :]
            fold["ld"][q] = inc("gp", "F%d" % (q % 2), lambda e, d=dst, s=src: e.dma_start(out=d, in_=s), amt=16)

        def fold_mm(r):
            q, hf = r // 2, r % 2
            wait("pe", "F%d" % (q % 2), fold["ld"][q])
            wait("pe", "CQ", g_consts)
            if r == 0:
                use("pe", "psB")
            else:
                wait("pe", "A", fold["cp"][r - 1])
            base = 4096 + (q % 2) * 2048 + hf * 1024
            s0 = q_sb[:, base:base + 512]
            s1 = q_sb[:, base + 512:base + 1024]
            emit("pe", lambda e, s=s0: e.matmul(psB[:, 0:512], bd64_s[:], s, start=True, stop=True))
            v = inc("pe", "P", lambda e, s=s1: e.matmul(psB[:, 512:1024], bd64_s[:], s, start=True, stop=True))
            fold["mm"][r] = v
            if hf == 1:
                release(("foldin", q % 2), "P", v)

        def fold_copy(r):
            if r < 2:
                wait("act", "P", last["scores_last"])
            wait("act", "P", fold["mm"][r])
            if r >= 2:
                wait("act", "FS%d" % (r % 2), fold["stv"][r - 2])
            dst = q_sb[:, (r % 2) * 1024:(r % 2) * 1024 + 1024]
            fold["cp"][r] = inc("act", "A",
                                lambda e, d=dst: e.activation(d, psB[:, 0:1024], AF.Copy, scale=FWHT_SCALE))
            if r == 127:
                release("psB", "A", fold["cp"][r])

        def fold_store(r):
            q, hf = r // 2, r % 2
            wait("sy", "A", fold["cp"][r])
            wp = w2e.rearrange("(m p) cols -> m p cols", p=128)
            v = 0
            for eh in range(2):
                src = q_sb[eh * 64:eh * 64 + 64, (r % 2) * 1024:(r % 2) * 1024 + 1024].rearrange(
                    "b (m dl) -> b m dl", m=8)
                dst = wp[8 * hf:8 * hf + 8, 2 * q + eh, :].rearrange(
                    "m (b dl) -> m b dl", b=64).transpose([1, 0, 2])
                v = inc("sy", "FS%d" % (r % 2), lambda e, d=dst, s=src: e.dma_start(out=d, in_=s), amt=16)
            fold["stv"][r] = v
            fold["last_store"] = v

        def emit_xt_load(g):
            use("sy", "xT")
            xdst = xT_sb.ap().rearrange("p (c t) -> p c t", c=16)
            xsrc = xTg[:, g * T:g * T + T].rearrange("(c p) t -> p c t", p=128)
            return inc("sy", "XT", lambda e, d=xdst, s=xsrc: e.dma_start(out=d, in_=s), amt=16)

        def emit_ln_part1(g, xT_ld):
            wait("pe", "V", v_consts)
            use("pe", "psA0")
            use("pe", "psA1")
            wait("pe", "XT", xT_ld)
            wait("gp", "XT", xT_ld)
            use("gp", ("accp",))
            gsum = 0
            stats_mm = 0
            for c in range(16):
                src = xT_sb[:, c * T:(c + 1) * T]
                if c == 0:
                    gsum = inc("gp", "PL", lambda e, s=src: e.tensor_copy(accp[:], s))
                else:
                    drain("gp")
                    gsum = inc("gp", "PL", lambda e, s=src: e.tensor_add(accp[:], accp[:], s))

                wait("act", "XT", xT_ld)
                use("act", ("xsq", c % 2))
                dst = xsq_t[:, (c % 2) * T:(c % 2) * T + T]
                sq_a = inc("act", "A", lambda e, d=dst, s=src: e.activation(d, s, AF.Square))

                wait("pe", "A", sq_a)
                sqc = xsq_t[:, (c % 2) * T:(c % 2) * T + T]
                stats_mm = inc("pe", "P", lambda e, s=sqc, st=(c == 0), sp=(c == 15):
                               e.matmul(psA1, ones128[:], s, start=st, stop=sp,
                                        skip_group_check=True))
                release(("xsq", c % 2), "P", stats_mm)
            wait("gp", "PL", gsum)
            drain("gp")
            use("gp", ("psbf",))
            gcast = inc("gp", "PL", lambda e: e.tensor_copy(psbf[:], accp[:]))
            release(("accp",), "PL", gcast)
            wait("pe", "PL", gcast)
            stats_mm = inc("pe", "P", lambda e: e.matmul(psA0, ones128[:], psbf[:],
                                                         start=True, stop=True))
            release(("psbf",), "P", stats_mm)
            last["stats_mm"] = stats_mm

            wait("dve", "P", stats_mm)
            use("dve", ("sscr", 0))
            emit("dve", lambda e: e.tensor_scalar_mul(sscr[:, 0:T], psA0, 1.0 / D))
            drain("dve")
            emit("dve", lambda e: e.tensor_mul(sscr[:, 0:T], sscr[:, 0:T], sscr[:, 0:T]))
            drain("dve")
            vv = inc("dve", "V", lambda e: e.scalar_tensor_tensor(
                sscr[:, 0:T], psA1, 1.0 / D, sscr[:, 0:T], ALU.mult, ALU.subtract))
            release("psA1", "V", vv)

            wait("act", "V", vv)
            use("act", ("accp",))
            emit("act", lambda e: e.activation(accp[:], sscr[:, 0:T], AF.Ln, bias=eps128[:]))
            drain("act")
            av = inc("act", "A", lambda e: e.activation(accp[:], accp[:], AF.Exp, scale=-0.5))
            release(("sscr", 0), "A", av)
            return av

        def emit_ln_part2(g, rstd_v):
            wait("dve", "A", rstd_v)
            wait("dve", "P", last["u_mm_last"])
            for c in range(16):
                xc = xT_sb[:, c * T:(c + 1) * T]
                xnc = xn_sb[:, c * T:(c + 1) * T]
                use("dve", ("sscr", c % 2))
                ts = sscr[:, (c % 2) * T:(c % 2) * T + T]
                emit("dve", lambda e, a1=ts, b1=xc: e.scalar_tensor_tensor(
                    a1, psA0, -1.0 / D, b1, ALU.mult, ALU.add))
                drain("dve")
                vv = inc("dve", "V", lambda e, d=xnc, a1=ts: e.tensor_mul(d, a1, accp[:]))
                drain("dve")
            last["xn_ready"] = vv
            release("psA0", "V", vv)
            release(("accp",), "V", vv)
            release(("sscr", 0), "V", vv)
            release(("sscr", 1), "V", vv)

        for g in range(NG):
            gcol = g * T

            if g == 0:
                xt0 = emit_xt_load(0)
                nbv = emit_ln_part1(0, xt0)
                emit_ln_part2(0, nbv)

            for i in range(16):
                if g == 0:
                    d = wq_b[i * 128:(i + 1) * 128, :].rearrange("p (c icol) -> p c icol", c=16)
                    s = wqT[:, i * 128:(i + 1) * 128].rearrange("(c p) icol -> p c icol", p=128)
                    cv["wq"] = inc("gp", "XW", lambda e, d=d, s=s: e.dma_start(out=d, in_=s), amt=16)
                dst = wbuf[:, (i % 4) * 2048:(i % 4) * 2048 + 2048]
                use("gp", ("wslot", i % 4))
                wait("gp", "XW", cv["wq"])
                ldsem = "W%d" % (i % 4)
                ld = inc("gp", ldsem, lambda e, d=dst, s=wq_b[i * 128:(i + 1) * 128, :]:
                         e.dma_start(out=d, in_=s), amt=16)

                pst = psC if i % 2 == 0 else psD
                pkey = "psC" if i % 2 == 0 else "psD"
                wait("pe", ldsem, ld)
                wait("pe", "V", last["xn_ready"])
                use("pe", pkey)
                mm = 0
                for c in range(16):
                    lt = wbuf[:, (i % 4) * 2048 + c * 128:(i % 4) * 2048 + c * 128 + 128]
                    rh = xn_sb[:, c * T:(c + 1) * T]
                    if c < 15:
                        emit("pe", lambda e, l=lt, r=rh, st=(c == 0), p=pst:
                             e.matmul(p[:], l, r, start=st, stop=False))
                    else:
                        mm = inc("pe", "P", lambda e, l=lt, r=rh, p=pst:
                                 e.matmul(p[:], l, r, start=False, stop=True))
                release(("wslot", i % 4), "P", mm)

                wait("act", "P", mm)
                if i == 0:
                    wait("act", "P", last["w2e_mm_last"])
                qd = q_sb[:, i * T:(i + 1) * T]
                av = inc("act", "A", lambda e, d=qd, p=pst: e.copy(d, p[:]))
                last["q_evac"][i] = av
                release(pkey, "A", av)

            for h in range(16):
                ks, vs = KVS[(2 * h) % 3], KVS[(2 * h + 1) % 3]
                if g == 0:
                    cv["k"] = inc("gp", "XK", lambda e, d=k_b[h * 128:(h + 1) * 128, :],
                                  s=kT[h * 128:(h + 1) * 128, :]: e.dma_start(out=d, in_=s), amt=16)
                use("gp", ("kv", (2 * h) % 3))
                wait("gp", "XK", cv["k"])
                ksem = "K%d" % ((2 * h) % 3)
                k_ld = inc("gp", ksem,
                           lambda e, d=arena[:, ks:ks + 4096], s=k_b[h * 128:(h + 1) * 128, :]:
                           e.dma_start(out=d, in_=s), amt=16)
                if g == 0:
                    d = v_b[h * 128:(h + 1) * 128, :].rearrange("p (c dv) -> p c dv", c=32)
                    s = vN[:, h * 128:(h + 1) * 128].rearrange("(c p) dv -> p c dv", p=128)
                    cv["v"] = inc("gp", "XV", lambda e, d=d, s=s: e.dma_start(out=d, in_=s), amt=16)
                use("gp", ("kv", (2 * h + 1) % 3))
                wait("gp", "XV", cv["v"])
                vsem = "K%d" % ((2 * h + 1) % 3)
                v_ld = inc("gp", vsem,
                           lambda e, d=arena[:, vs:vs + 4096], s=v_b[h * 128:(h + 1) * 128, :]:
                           e.dma_start(out=d, in_=s), amt=16)

                qh = q_sb[:, h * T:(h + 1) * T]
                scores_done = {}
                exp_done = {}
                pv_done = {}

                def emit_tail_A(hp):
                    wait("act", "P", last["pv15"][hp])
                    if hp == 0:
                        wait("act", "P", last["wo_mm_last"])
                    cx2 = arena[:, CTX0 + hp * T:CTX0 + hp * T + T]
                    cev = inc("act", "A", lambda e, d=cx2: e.copy(d, psC[:]))
                    release("psC", "A", cev)

                def emit_tail_B(hp):
                    pde = psD if hp % 2 == 0 else psE
                    pdekey = "psD" if hp % 2 == 0 else "psE"
                    use("act", ("accp",))
                    emit("act", lambda e, p=pde: e.activation(accp[:], p[:], AF.Ln))
                    drain("act")
                    lv = inc("act", "A", lambda e: e.activation(accp[:], accp[:], AF.Exp, scale=-1.0))
                    last["lnexp"][hp] = lv
                    release(pdekey, "A", lv)

                def emit_scores(b, h=h, ks=ks, qh=qh, scores_done=scores_done):
                    pab = psA if b % 2 == 0 else psB
                    if b % 2 == 0:
                        use("pe", "psA0")
                        use("pe", "psA1")
                    else:
                        use("pe", "psB")
                    vv = 0
                    for u in range(2):
                        cchunk = 2 * b + u
                        lt = arena[:, ks + cchunk * 128:ks + cchunk * 128 + 128]
                        if u == 0:
                            emit("pe", lambda e, l=lt, p=pab: e.matmul(p[:, 0:512], l, qh, start=True, stop=True))
                        else:
                            vv = inc("pe", "P", lambda e, l=lt, p=pab:
                                     e.matmul(p[:, 512:1024], l, qh, start=True, stop=True))
                    scores_done[b] = vv

                def emit_exp(b, h=h, scores_done=scores_done, exp_done=exp_done):
                    wait("act", "P", scores_done[b])
                    use("act", ("P", b % 2))
                    pab = psA if b % 2 == 0 else psB
                    pdst = arena[:, P0 + (b % 2) * 1024:P0 + (b % 2) * 1024 + 1024]
                    ev = inc("act", "A", lambda e, p=pab, d=pdst:
                             e.activation(d, p[:, 0:1024], AF.Exp, scale=SCALE))
                    exp_done[b] = ev
                    if b % 2 == 0:
                        release("psA0", "A", ev)
                        release("psA1", "A", ev)
                    else:
                        release("psB", "A", ev)

                pair_v = {}
                quad_v = {}

                def emit_pair(b, h=h, exp_done=exp_done, pair_v=pair_v, quad_v=quad_v):
                    pb = P0 + (b % 2) * 1024
                    c0 = arena[:, pb:pb + 512]
                    c1 = arena[:, pb + 512:pb + 1024]
                    wait("dve", "A", exp_done[b])
                    use("dve", ("prb", b % 3))
                    dst = (prb[:, 0:512], prb[:, 512:1024], qrb[:, 0:512])[b % 3]
                    pair_v[b] = inc("dve", "V", lambda e, d=dst, a1=c0, b1=c1: e.tensor_add(d, a1, b1))

                def emit_pv(b, h=h, vs=vs, v_ld=v_ld, vsem=vsem, exp_done=exp_done, pv_done=pv_done, pair_v=pair_v):
                    wait("pe", "A", exp_done[b])
                    if b == 0:
                        wait("pe", vsem, v_ld)
                        use("pe", "psC")
                    pb = P0 + (b % 2) * 1024
                    for u in range(2):
                        cchunk = 2 * b + u
                        lt = arena[:, vs + cchunk * 128:vs + cchunk * 128 + 128]
                        rh = arena[:, pb + u * 512:pb + u * 512 + 512]
                        st = (b == 0 and u == 0)
                        sp = (b == 15 and u == 1)
                        if u == 0:
                            emit("pe", lambda e, l=lt, r=rh, st=st, sp=sp:
                                 e.matmul(psC[:], l, r, start=st, stop=sp, skip_group_check=True))
                        else:
                            vv = inc("pe", "P", lambda e, l=lt, r=rh, st=st, sp=sp:
                                     e.matmul(psC[:], l, r, start=st, stop=sp, skip_group_check=True))
                    if b >= 1:
                        pbb = b - 1
                        if b == 1:
                            use("pe", "psD" if h % 2 == 0 else "psE")
                        wait("pe", "V", pair_v[pbb])
                        rp = (prb[:, 0:512], prb[:, 512:1024], qrb[:, 0:512])[pbb % 3]
                        pde = psD if h % 2 == 0 else psE
                        dv3 = inc("pe", "P", lambda e, r=rp, p=pde, st=(pbb == 0):
                                  e.matmul(p[:], ones128[:], r, start=st, stop=False,
                                           skip_group_check=True))
                        release(("prb", pbb % 3), "P", dv3)
                    pv_done[b] = vv
                    release_multi(("P", b % 2), [("P", vv), ("V", pair_v[b])])

                if h > 0:
                    emit_tail_A(h - 1)
                wait("pe", "A", last["q_evac"][h])
                wait("pe", ksem, k_ld)
                emit_scores(0)
                emit_exp(0)
                emit_pair(0)
                emit_scores(1)
                emit_exp(1)
                emit_pair(1)
                if h > 0:
                    emit_tail_B(h - 1)
                for b in range(2, 16):
                    emit_pv(b - 2)
                    if b == 4 and h > 0:
                        wait("dve", "A", last["lnexp"][h - 1])
                        cx = arena[:, CTX0 + (h - 1) * T:CTX0 + (h - 1) * T + T]
                        mv = inc("dve", "V", lambda e, c=cx: e.tensor_mul(c, c, accp[:]))
                        last["ctx_mul"][h - 1] = mv
                        release(("accp",), "V", mv)
                    emit_scores(b)
                    emit_exp(b)
                    emit_pair(b)
                emit_pv(14)
                emit_pv(15)
                wait("pe", "V", pair_v[15])
                pde = psD if h % 2 == 0 else psE
                dfin = inc("pe", "P", lambda e, r=(prb[:, 0:512], prb[:, 512:1024], qrb[:, 0:512])[15 % 3], p=pde:
                           e.matmul(p[:], ones128[:], r, start=False, stop=True,
                                    skip_group_check=True))
                release(("prb", 15 % 3), "P", dfin)
                last["pv15"][h] = dfin
                last["scores_last"] = scores_done[15]
                release(("kv", (2 * h) % 3), "P", scores_done[15])
                release(("kv", (2 * h + 1) % 3), "P", pv_done[15])
                if h == 15:
                    emit_tail_A(15)
                    emit_tail_B(15)

            wait("dve", "A", last["lnexp"][15])
            cx = arena[:, CTX0 + 15 * T:CTX0 + 15 * T + T]
            mv = inc("dve", "V", lambda e, c=cx: e.tensor_mul(c, c, accp[:]))
            last["ctx_mul"][15] = mv
            release(("accp",), "V", mv)

            for m in range(16):
                if g == 0:
                    d = wo_b[m * 128:(m + 1) * 128, :].rearrange("p (c icol) -> p c icol", c=16)
                    s = woT[:, m * 128:(m + 1) * 128].rearrange("(c p) icol -> p c icol", p=128)
                    cv["wo"] = inc("gp", "XO", lambda e, d=d, s=s: e.dma_start(out=d, in_=s), amt=16)
                dst = wbuf[:, (m % 4) * 2048:(m % 4) * 2048 + 2048]
                use("gp", ("wslot", m % 4))
                wait("gp", "XO", cv["wo"])
                ldsem = "W%d" % (m % 4)
                ld = inc("gp", ldsem, lambda e, d=dst, s=wo_b[m * 128:(m + 1) * 128, :]:
                         e.dma_start(out=d, in_=s), amt=16)

                pst = psC if m % 2 == 0 else psD
                pkey = "psC" if m % 2 == 0 else "psD"
                wait("pe", ldsem, ld)
                wait("pe", "V", last["ctx_mul"][15])
                use("pe", pkey)
                mm = 0
                for c in range(16):
                    lt = wbuf[:, (m % 4) * 2048 + c * 128:(m % 4) * 2048 + c * 128 + 128]
                    rh = arena[:, CTX0 + c * T:CTX0 + c * T + T]
                    if c < 15:
                        emit("pe", lambda e, l=lt, r=rh, st=(c == 0), p=pst:
                             e.matmul(p[:], l, r, start=st, stop=False))
                    else:
                        mm = inc("pe", "P", lambda e, l=lt, r=rh, p=pst:
                                 e.matmul(p[:], l, r, start=False, stop=True))
                last["wo_mm_last"] = mm
                release(("wslot", m % 4), "P", mm)

                wait("dve", "P", mm)
                rd = res_sb[:, m * T:(m + 1) * T]
                xm = xT_sb[:, m * T:(m + 1) * T]
                vv = inc("dve", "V", lambda e, d=rd, a1=xm, p=pst: e.tensor_add(d, a1, p[:]))
                last["wo_evac_last"] = vv
                release(pkey, "V", vv)
            release("xT", "V", last["wo_evac_last"])

            if g + 1 < NG:
                xt_next = emit_xt_load(g + 1)

            hid_v = {}
            h128_done = {}
            hid1_ev = {}

            def emit_h128(jp, hid_v=hid_v, h128_done=h128_done, hid1_ev=hid1_ev):
                wait("pe", "V", hid_v[jp])
                use("pe", "psE")
                hv = inc("pe", "P", lambda e, s=hid_t[:, (jp % 2) * T:(jp % 2) * T + T]:
                         e.matmul(psE[:], h128_s[:], s, start=True, stop=True))
                h128_done[jp] = hv
                release(("hid", jp % 2), "P", hv)
                wait("dve", "P", hv)
                dd = arena[:, jp * T:(jp + 1) * T]
                ev = inc("dve", "V", lambda e, d=dd: e.tensor_copy(d, psE[:]))
                hid1_ev[jp] = ev
                release("psE", "V", ev)

            for j in range(64):
                if g == 0:
                    d = w1_b[j * 128:(j + 1) * 128, :].rearrange("p (c hcol) -> p c hcol", c=16)
                    s = w1Tp[:, j * 128:(j + 1) * 128].rearrange("(c p) hcol -> p c hcol", p=128)
                    cv["w1"] = inc("gp", "X1", lambda e, d=d, s=s: e.dma_start(out=d, in_=s), amt=16)
                    d = w3_b[j * 128:(j + 1) * 128, :].rearrange("p (c hcol) -> p c hcol", c=16)
                    s = w3Tp[:, j * 128:(j + 1) * 128].rearrange("(c p) hcol -> p c hcol", p=128)
                    cv["w3"] = inc("gp", "X3", lambda e, d=d, s=s: e.dma_start(out=d, in_=s), amt=16)
                use("gp", ("wslot", j % 2))
                wait("gp", "X1", cv["w1"])
                d1 = wbuf[:, (j % 2) * 2048:(j % 2) * 2048 + 2048]
                ld1 = inc("gp", "W%d" % (j % 2), lambda e, d=d1, s=w1_b[j * 128:(j + 1) * 128, :]:
                          e.dma_start(out=d, in_=s), amt=16)
                d3 = wbuf[:, (2 + j % 2) * 2048:(2 + j % 2) * 2048 + 2048]
                if g == 0:
                    use("gp", ("wslot", 2 + j % 2))
                    wait("gp", "X3", cv["w3"])
                    ld3sem = "W%d" % (2 + j % 2)
                    ld3 = inc("gp", ld3sem, lambda e, d=d3, s=w3_b[j * 128:(j + 1) * 128, :]:
                              e.dma_start(out=d, in_=s), amt=16)
                else:
                    use("act", ("wslot", 2 + j % 2))
                    ld3sem = "Y%d" % (2 + j % 2)
                    ld3 = inc("act", ld3sem, lambda e, d=d3, s=w3_b[j * 128:(j + 1) * 128, :]:
                              e.dma_start(out=d, in_=s), amt=16)
                if g == 0:
                    fold_load(j)

                pg = psC[:] if j % 2 == 0 else psA0
                pu = psD[:] if j % 2 == 0 else psA1
                gkey = "psC" if j % 2 == 0 else "psA0"
                ukey = "psD" if j % 2 == 0 else "psA1"

                wait("pe", "W%d" % (j % 2), ld1)
                use("pe", gkey)
                gv = 0
                for c in range(16):
                    lt = wbuf[:, (j % 2) * 2048 + c * 128:(j % 2) * 2048 + c * 128 + 128]
                    rh = xn_sb[:, c * T:(c + 1) * T]
                    if c < 15:
                        emit("pe", lambda e, l=lt, r=rh, st=(c == 0), p=pg:
                             e.matmul(p, l, r, start=st, stop=False))
                    else:
                        gv = inc("pe", "P", lambda e, l=lt, r=rh, p=pg:
                                 e.matmul(p, l, r, start=False, stop=True))
                release(("wslot", j % 2), "P", gv)

                wait("pe", ld3sem, ld3)
                use("pe", ukey)
                uv = 0
                for c in range(16):
                    lt = wbuf[:, (2 + j % 2) * 2048 + c * 128:(2 + j % 2) * 2048 + c * 128 + 128]
                    rh = xn_sb[:, c * T:(c + 1) * T]
                    if c < 15:
                        emit("pe", lambda e, l=lt, r=rh, st=(c == 0), p=pu:
                             e.matmul(p, l, r, start=st, stop=False))
                    else:
                        uv = inc("pe", "P", lambda e, l=lt, r=rh, p=pu:
                                 e.matmul(p, l, r, start=False, stop=True))
                release(("wslot", 2 + j % 2), "P", uv)
                last["u_mm_last"] = uv

                if j >= 1:
                    emit_h128(j - 1)
                if g == 0:
                    fold_mm(2 * j)
                    fold_copy(2 * j)
                    fold_mm(2 * j + 1)
                    fold_copy(2 * j + 1)
                    fold_store(2 * j)
                    fold_store(2 * j + 1)

                wait("act", "P", gv)
                use("act", ("sscr", j % 2))
                sd = sscr[:, (j % 2) * T:(j % 2) * T + T]
                sv = inc("act", "A", lambda e, d=sd, p=pg: e.activation(d, p, AF.Silu))
                release(gkey, "A", sv)

                wait("dve", "A", sv)
                wait("dve", "P", uv)
                use("dve", ("hid", j % 2))
                hd = hid_t[:, (j % 2) * T:(j % 2) * T + T]
                hvv = inc("dve", "V", lambda e, d=hd, s=sd, p=pu: e.tensor_tensor(d, s, p, ALU.mult))
                hid_v[j] = hvv
                release(ukey, "V", hvv)
                release(("sscr", j % 2), "V", hvv)

            emit_h128(63)

            mmv = 0
            for m in range(16):
                if g == 0 and m == 0:
                    wait("act", "FS0", fold["stv"][126])
                    wait("act", "FS1", fold["stv"][127])
                if m % 2 == 0:
                    for s in range(4):
                        use("act", ("wslot", s))
                    dst = wbuf[:, 0:8192]
                    buf = wbuf
                else:
                    use("act", ("qsb_w2e",))
                    wait("act", "P", last["scores_last"])
                    if g == 0:
                        wait("act", "P", fold["mm"][127])
                    dst = q_sb[:, 0:8192]
                    buf = q_sb
                ld = inc("act", "WE%d" % (m % 2), lambda e, d=dst, s=w2e[m * 128:(m + 1) * 128, :]:
                         e.dma_start(out=d, in_=s), amt=16)

                pst = psC if m % 2 == 0 else psD
                pkey = "psC" if m % 2 == 0 else "psD"
                wait("pe", "WE%d" % (m % 2), ld)
                use("pe", pkey)
                for j in range(64):
                    lt = buf[:, j * 128:j * 128 + 128]
                    rh = arena[:, j * T:(j + 1) * T]
                    wait("pe", "V", hid1_ev[j])
                    if j < 63:
                        emit("pe", lambda e, l=lt, r=rh, st=(j == 0), p=pst:
                             e.matmul(p[:], l, r, start=st, stop=False))
                    else:
                        mmv = inc("pe", "P", lambda e, l=lt, r=rh, p=pst:
                                  e.matmul(p[:], l, r, start=False, stop=True))
                last["w2e_mm_last"] = mmv
                if m % 2 == 0:
                    for s in range(4):
                        release(("wslot", s), "P", mmv)
                else:
                    release(("qsb_w2e",), "P", mmv)

                if m == 7 and g + 1 < NG:
                    nbv_next = emit_ln_part1(g + 1, xt_next)
                if m == 12 and g + 1 < NG:
                    emit_ln_part2(g + 1, nbv_next)

                wait("dve", "P", mmv)
                use("dve", ("outb", m % 2))
                rm = res_sb[:, m * T:(m + 1) * T]
                ob = outb[:, (m % 2) * T:(m % 2) * T + T]
                vv = inc("dve", "V", lambda e, d=ob, a1=rm, p=pst: e.tensor_add(d, a1, p[:]))
                last["epi_last"] = vv
                release(pkey, "V", vv)

                wait("sy", "V", vv)
                dsto = outT[m * 128:(m + 1) * 128, gcol:gcol + T]
                sv = inc("sy", "OB%d" % (m % 2), lambda e, d=dsto, s=ob: e.dma_start(out=d, in_=s), amt=16)
                release(("outb", m % 2), "OB%d" % (m % 2), sv)

            for s in range(3):
                release(("kv", s), "P", mmv)
            release(("P", 0), "P", mmv)
            release(("P", 1), "P", mmv)

        emit("sy", lambda e, v=cnt["OB0"]: e.wait_ge(SEM["OB0"], v))
        emit("sy", lambda e, v=cnt["OB1"]: e.wait_ge(SEM["OB1"], v))

        with nc.Block() as block:
            @block.tensor
            def _(e):
                for f in EV["pe"]:
                    f(e)

            @block.scalar
            def _(e):
                for f in EV["act"]:
                    f(e)

            @block.vector
            def _(e):
                for f in EV["dve"]:
                    f(e)

            @block.gpsimd
            def _(e):
                for f in EV["gp"]:
                    f(e)

            @block.sync
            def _(e):
                for f in EV["sy"]:
                    f(e)

    return nc


def kernel(**inputs):
    from concourse.bass_utils import run_bass_kernel_spmd
    xT, shared = prep_host(inputs)
    nc = build(NG=8)
    in_maps = []
    for c in range(8):
        m = dict(shared)
        m["xT"] = np.ascontiguousarray(xT[:, c * NC_TOK:(c + 1) * NC_TOK])
        in_maps.append(m)
    res = run_bass_kernel_spmd(nc, in_maps, core_ids=list(range(8)))
    outT = np.concatenate([res.results[c]["outT"] for c in range(8)], axis=1)
    return np.ascontiguousarray(outT.T).reshape(2, 16384, 2048)
